# version 02
# baseline (speedup 1.0000x reference)
import os
import numpy as np
import ml_dtypes
import contextlib
import concourse.bass as bass
import concourse.mybir as mybir
from concourse.bass_utils import run_bass_kernel_spmd

F32 = mybir.dt.float32
BF16 = mybir.dt.bfloat16
AL = mybir.AluOpType
ACT = mybir.ActivationFunctionType
AX = mybir.AxisListType
BFD = ml_dtypes.bfloat16


F32 = mybir.dt.float32
BF16 = mybir.dt.bfloat16
AL = mybir.AluOpType
ACT = mybir.ActivationFunctionType
AX = mybir.AxisListType
BFD = ml_dtypes.bfloat16

TOK = 1024
HALO = 3
TOKH = TOK + HALO
NCH = 16
NH = 16
FFH = 5632
FHC = FFH // 128
PROJ_TILES = [(0, 512), (512, 512), (1024, 3)]
SCALE = float(128 ** -0.5)


def fm(hbm_ap):
    return hbm_ap.rearrange("(c p) t -> p c t", p=128)


def build_launch_A(debug=(), stop_after=None):
    nc = bass.Bass("TRN2", target_bir_lowering=False, debug=False)
    DT = nc.dram_tensor
    ph = P(nc)

    xT = DT("xT", [2048, TOKH], F32, kind="ExternalInput").ap()
    WqT = DT("WqT", [2048, 2048], F32, kind="ExternalInput").ap()
    WkT = DT("WkT", [2048, 2048], F32, kind="ExternalInput").ap()
    WvT = DT("WvT", [2048, 2048], F32, kind="ExternalInput").ap()
    convw = DT("convw", [128, 3, NCH, 4], F32, kind="ExternalInput").ap()
    WsvT = DT("WsvT", [128, 128], F32, kind="ExternalInput").ap()
    A_log_r = DT("A_log_r", [1, 16], F32, kind="ExternalInput").ap()
    dt_b = DT("dt_b", [128, 16], F32, kind="ExternalInput").ap()
    W1r = DT("W1r", [128, 2, 32], F32, kind="ExternalInput").ap()
    b1c = DT("b1c", [32, 2], F32, kind="ExternalInput").ap()
    W2c = DT("W2c", [32, 2], F32, kind="ExternalInput").ap()
    b2r = DT("b2r", [128, 2], F32, kind="ExternalInput").ap()
    Wg1T = DT("Wg1T", [2048, 128], F32, kind="ExternalInput").ap()
    Wg2T = DT("Wg2T", [128, 2048], F32, kind="ExternalInput").ap()
    bg2c = DT("bg2c", [128, 16], F32, kind="ExternalInput").ap()
    w1T = DT("w1T", [2048, FFH], F32, kind="ExternalInput").ap()
    w2T = DT("w2T", [2048, FFH], F32, kind="ExternalInput").ap()
    w3T = DT("w3T", [FFH, 2048], F32, kind="ExternalInput").ap()

    o_kn = DT("o_kn", [2048, TOK], BF16, kind="ExternalOutput").ap()
    o_qs = DT("o_qs", [2048, TOK], BF16, kind="ExternalOutput").ap()
    o_v = DT("o_v", [2048, TOK], BF16, kind="ExternalOutput").ap()
    o_g = DT("o_g", [2048, TOK], F32, kind="ExternalOutput").ap()
    o_beta = DT("o_beta", [NH, TOK], F32, kind="ExternalOutput").ap()
    o_sgate = DT("o_sgate", [2048, TOK], BF16, kind="ExternalOutput").ap()
    o_acc = DT("o_acc", [2048, TOK], F32, kind="ExternalOutput").ap()
    dbg = {}
    dshp = {"xn": [2048, TOKH], "kraw": [2048, TOK], "lrow": [4, 512], "M": [128, TOK], "BR": [128, TOK], "hid": [32, 2, TOK], "thr": [128, 8], "hb": [128, FHC * 512], "gac": [128, 4, FHC], "wq3": [128, 2, 2048], "hb1": [128, FHC * 512], "wq1": [128, 2, 2048]}
    for name in debug:
        dbg[name] = DT("dbg_" + name, dshp[name], F32, kind="ExternalOutput").ap()

    wq12s = DT("wq12s", [2, 2048, FFH], BF16, kind="Internal").ap()
    wq3s = DT("wq3s", [FFH, 2048], BF16, kind="Internal").ap()

    CVALS = [0.0, 1.0, -1.0, 1e-5, 1e-6, 1e-12, 1.0 / 2048,
             0.5 / (2048.0 * FFH), 0.5e-8, SCALE]
    cval_t = nc.inline_tensor(np.tile(np.array(CVALS, np.float32), (128, 1)), name="cvals").ap()

    onec_f = nc.inline_tensor(np.ones((128, 1), np.float32), name="onec_f").ap()
    onec_b = nc.inline_tensor(np.ones((128, 1), BFD), name="onec_b").ap()
    oner_f = nc.inline_tensor(np.ones((128, 128), np.float32), name="oner_f").ap()
    oner_b = nc.inline_tensor(np.ones((128, 128), BFD), name="oner_b").ap()
    oner_bs = nc.inline_tensor(np.full((128, 128), SCALE, BFD), name="oner_bs").ap()

    import contextlib
    with contextlib.ExitStack() as es:
        E = es.enter_context
        cvs = E(nc.sbuf_tensor("cvs", [128, len(CVALS)], F32))
        xn = E(nc.sbuf_tensor("xn", [128, NCH, TOKH], BF16))
        onecf = E(nc.sbuf_tensor("onecf", [128, 1], F32))
        onecb = E(nc.sbuf_tensor("onecb", [128, 1], BF16))
        onerf = E(nc.sbuf_tensor("onerf", [128, 128], F32))
        onerb = E(nc.sbuf_tensor("onerb", [128, 128], BF16))
        onerbs = E(nc.sbuf_tensor("onerbs", [128, 128], BF16))
        convc = E(nc.sbuf_tensor("convc", [128, 3, NCH, 4], F32))
        wsv = E(nc.sbuf_tensor("wsv", [128, 128], BF16))
        dtb = E(nc.sbuf_tensor("dtb", [128, 16], F32))
        negeA = E(nc.sbuf_tensor("negeA", [128, 16], F32))
        mlw1 = E(nc.sbuf_tensor("mlw1", [128, 2, 32], BF16))
        mlb1 = E(nc.sbuf_tensor("mlb1", [32, 2], F32))
        mlw2 = E(nc.sbuf_tensor("mlw2", [32, 2], F32))
        mlb2 = E(nc.sbuf_tensor("mlb2", [128, 2], F32))
        wg = E(nc.sbuf_tensor("wg", [128, NCH, 128], BF16))
        g1o = E(nc.sbuf_tensor("g1o", [128, TOK], BF16))
        bgc = E(nc.sbuf_tensor("bgc", [128, 16], F32))
        sg = E(nc.sbuf_tensor("sg", [128, 2, 2, 512], BF16))
        pp = E(nc.psum_tensor("pp", [128, 8, 512], F32))
        block = E(nc.Block())
        pb = lambda i: pp.ap()[:, i, :]

        for vi, val in enumerate(CVALS):
            nc.const_aps.register(F32, float(val), cvs[:, vi:vi + 1])

        @block.sync
        def _(sy):
            ph.sigd(sy.dma_start(out=cvs[:], in_=cval_t[:]), "ld0")
            for dst, src in ((onecf, onec_f), (onecb, onec_b), (onerf, oner_f), (onerb, oner_b),
                             (onerbs, oner_bs), (convc, convw), (dtb, dt_b),
                             (mlb1, b1c), (mlw2, W2c), (mlb2, b2r)):
                ph.sigd(sy.dma_start(out=dst[:], in_=src[:]), "ld0")
        @block.gpsimd
        def _(gp):
            ph.sigd(gp.dma_start(out=wsv[:], in_=WsvT[:]), "ld0")
            ph.sigd(gp.dma_start(out=mlw1[:], in_=W1r[:]), "ld0")
        for emit in (block.tensor, block.vector, block.scalar, block.gpsimd):
            def _prelude(e):
                ph.wait(e, "ld0")
            emit(_prelude)

        with nc.sbuf_tensor("arow", [1, 32], F32) as arow:
            @block.sync
            def _(sy):
                ph.sigd(sy.dma_start(out=arow[0:1, 0:16], in_=A_log_r[:]), "lda")
            @block.scalar
            def _(sc):
                ph.wait(sc, "lda")
                ph.sig(sc.activation(arow[0:1, 16:32], arow[0:1, 0:16], ACT.Exp), "ea")
            @block.tensor
            def _(t):
                ph.wait(t, "ea", "ld0")
                ph.sig(t.matmul(pp.ap()[:, 0, 0:16], onerf[0:1, :], arow[0:1, 16:32],
                                start=True, stop=True), "eamm")
            @block.vector
            def _(v):
                ph.wait(v, "eamm")
                ph.sig(v.tensor_scalar_mul(negeA[:], pp.ap()[:, 0, 0:16], -1.0), "ead")

            emit_ln(nc, block, ph, pp, xn, xT, onecf, onerf, dbg)

            if stop_after != "ln":
                with (
                    nc.sbuf_tensor("qt", [128, NCH, TOK], BF16) as qt,
                    nc.sbuf_tensor("kt", [128, NCH, TOK], BF16) as kt,
                    nc.sbuf_tensor("vt", [128, NCH, TOK], BF16) as vt,
                ):
                    emit_proj_conv(nc, block, ph, pp, xn, qt, kt, vt, convc,
                                   WqT, WkT, WvT, o_v, dbg)
                    if stop_after != "proj":
                        emit_stats(nc, block, ph, pp, qt, kt, vt, wsv, onecb, onerf, onerb, onerbs,
                                   mlw1, mlb1, mlw2, mlb2, dtb, negeA, o_beta, o_g, o_kn, o_qs, dbg)
            do_gate = stop_after not in ("proj", "stats", "ln")
            do_ffn = do_gate and stop_after != "gate"

        if do_gate:
            emit_gate(nc, block, ph, pp, xn, Wg1T, Wg2T, bg2c, o_sgate, wg, g1o, sg, bgc)
        if do_ffn:
            emit_ffn(nc, block, ph, pp, xn, xT, o_acc, w1T, w2T, w3T,
                     wq12s, wq3s, onecf, onerf, dbg)

        @block.sync
        def _(sy):
            for nm in ("stv", "stq", "stk", "stbeta", "stacc"):
                if nm in ph.sems:
                    ph.wait(sy, nm)
        @block.gpsimd
        def _(gp):
            for nm in ("stg", "stsg"):
                if nm in ph.sems:
                    ph.wait(gp, nm)
    return nc


def emit_ln(nc, block, ph, pp, xn, xT, onecf, onerf, dbg):
        with (
            nc.sbuf_tensor("xs", [128, NCH, TOKH], F32) as xs,
            nc.sbuf_tensor("sq", [128, 2, 512], F32) as sq,
            nc.sbuf_tensor("lrow", [1, 4, 512], F32) as lrow,
            nc.sbuf_tensor("lb", [128, 2, 512], F32) as lb,
        ):
            @block.sync
            def _(sy):
                ph.sigd(sy.dma_start(out=xs[:], in_=fm(xT)), "ldx")

            for ti, (t0, tn) in enumerate([(0, 512), (512, 512), (515, 512)]):
                tg = f"ln{ti}"
                for c in range(NCH):
                    @block.scalar
                    def _(sc, c=c, t0=t0, tn=tn, tg=tg, ti=ti):
                        if c == 0:
                            ph.wait(sc, "ldx")
                            if ti > 0:
                                ph.wait(sc, f"ln{ti-1}xn")
                        if c >= 2:
                            ph.wait(sc, tg + f"mc{c-2}")
                        ph.sig(sc.activation(sq[:, c % 2, 0:tn], xs[:, c, t0:t0 + tn],
                                             ACT.Square), tg + f"sq{c}")
                    @block.tensor
                    def _(t, c=c, t0=t0, tn=tn, tg=tg, ti=ti):
                        if c == 0:
                            ph.wait(t, "ldx", "eamm")
                            if ti > 0:
                                ph.wait(t, f"ln{ti-1}bce")
                        t.matmul(pp.ap()[0:1, 0, 0:tn], onecf[:], xs[:, c, t0:t0 + tn],
                                 start=(c == 0), stop=(c == NCH - 1))
                        ph.wait(t, tg + f"sq{c}")
                        i = t.matmul(pp.ap()[0:1, 1, 0:tn], onecf[:], sq[:, c % 2, 0:tn],
                                     start=(c == 0), stop=(c == NCH - 1))
                        ph.sig(i, tg + f"mc{c}")
                        if c == NCH - 1:
                            ph.also(tg + "mm", tg + f"mc{c}")
                @block.scalar
                def _(sc, t0=t0, tn=tn, tg=tg):
                    ph.wait(sc, tg + "mm")
                    sc.mul(lrow[0:1, 0, 0:tn], pp.ap()[0:1, 0, 0:tn], 1.0 / 2048)
                    ph.sig(sc.mul(lrow[0:1, 1, 0:tn], pp.ap()[0:1, 1, 0:tn], 1.0 / 2048), tg + "mv")
                @block.vector
                def _(v, t0=t0, tn=tn, tg=tg):
                    ph.wait(v, tg + "mv")
                    v.scalar_tensor_tensor(lrow[0:1, 2, 0:tn], lrow[0:1, 0, 0:tn], 0.0,
                                           lrow[0:1, 1, 0:tn], AL.mult, AL.subtract) \
                        if False else None
                    v.tensor_mul(lrow[0:1, 2, 0:tn], lrow[0:1, 0, 0:tn], lrow[0:1, 0, 0:tn])
                    ph.sig(v.tensor_sub(lrow[0:1, 2, 0:tn], lrow[0:1, 1, 0:tn], lrow[0:1, 2, 0:tn]),
                           tg + "var")
                @block.scalar
                def _(sc, t0=t0, tn=tn, tg=tg):
                    ph.wait(sc, tg + "var")
                    ph.sig(sc.activation(lrow[0:1, 3, 0:tn], lrow[0:1, 2, 0:tn], ACT.Sqrt,
                                         bias=1e-5), tg + "sd")
                @block.vector
                def _(v, t0=t0, tn=tn, tg=tg):
                    ph.wait(v, tg + "sd")
                    v.reciprocal(lrow[0:1, 1, 0:tn], lrow[0:1, 3, 0:tn])
                    ph.sig(v.tensor_mul(lrow[0:1, 2, 0:tn], lrow[0:1, 0, 0:tn],
                                        lrow[0:1, 1, 0:tn]), tg + "rs")
                @block.tensor
                def _(t, t0=t0, tn=tn, tg=tg):
                    ph.wait(t, tg + "rs")
                    t.matmul(pp.ap()[:, 2, 0:tn], onerf[0:1, :], lrow[0:1, 1, 0:tn], start=True, stop=True)
                    ph.sig(t.matmul(pp.ap()[:, 3, 0:tn], onerf[0:1, :], lrow[0:1, 2, 0:tn],
                                    start=True, stop=True), tg + "bc")
                @block.vector
                def _(v, t0=t0, tn=tn, tg=tg):
                    ph.wait(v, tg + "bc")
                    v.tensor_copy(lb[:, 0, 0:tn], pp.ap()[:, 2, 0:tn])
                    ph.sig(v.tensor_copy(lb[:, 1, 0:tn], pp.ap()[:, 3, 0:tn]), tg + "bce")
                    for c in range(NCH):
                        v.tensor_mul(sq[:, c % 2, 0:tn], xs[:, c, t0:t0 + tn], lb[:, 0, 0:tn])
                        i = v.tensor_sub(xn[:, c, t0:t0 + tn], sq[:, c % 2, 0:tn], lb[:, 1, 0:tn])
                    ph.sig(i, tg + "xn")

            if "lrow" in dbg:
                @block.gpsimd
                def _(gp):
                    ph.wait(gp, "ln2rs")
                    ph.sigd(gp.dma_start(out=dbg["lrow"][None], in_=lrow[0:1, :, :]), "dlr")
                    ph.wait(gp, "dlr")
            if "xn" in dbg:
                @block.gpsimd
                def _(gp):
                    ph.wait(gp, "ln0xn", "ln1xn", "ln2xn")
                    ph.sigd(gp.dma_start(out=fm(dbg["xn"]), in_=xn[:]), "dx")
                    ph.wait(gp, "dx")


def emit_proj_conv(nc, block, ph, pp, xn, qt, kt, vt, convc, WqT, WkT, WvT, o_v, dbg):
    with (
        nc.sbuf_tensor("wblk", [128, 2, NCH, 256], BF16) as wblk,
        nc.sbuf_tensor("raw", [128, 4, TOKH], BF16) as raw,
        nc.sbuf_tensor("cacc", [128, 2, TOK], F32) as cacc,
    ):
        projs = [(WqT, qt, 0, "q"), (WkT, kt, 1, "k"), (WvT, vt, 2, "v")]
        order = [(pi, ocg) for pi in range(3) for ocg in range(8)]
        for n_, (pi, ocg) in enumerate(order):
            W_, dst, ci, pnm = projs[pi]
            bsel = n_ % 2
            @block.gpsimd
            def _(gp, W_=W_, ocg=ocg, bsel=bsel, n_=n_):
                if n_ < 2:
                    ph.wait(gp, "ln2xn")
                if n_ >= 2:
                    pi2, ocg2 = order[n_ - 2]
                    ph.wait(gp, f"mm{projs[pi2][3]}{ocg2}")
                ph.sigd(gp.dma_start(
                    out=wblk[:, bsel],
                    in_=W_.rearrange("(c p) o -> p c o", p=128)[:, :, ocg * 256:(ocg + 1) * 256]),
                    f"wl{n_}")
            @block.tensor
            def _(t, ocg=ocg, bsel=bsel, pnm=pnm, n_=n_):
                ph.wait(t, f"wl{n_}")
                if n_ == 0:
                    ph.wait(t, "ln0xn", "ln1xn", "ln2xn")
                if n_ >= 1:
                    pi2, ocg2 = order[n_ - 1]
                    ph.wait(t, f"cv{projs[pi2][3]}{ocg2}")
                for oci in range(2):
                    for kc in range(NCH):
                        t.ldweights(wblk[:, bsel, kc, oci * 128:(oci + 1) * 128])
                        for ti, (t0, tn) in enumerate(PROJ_TILES):
                            out = (pp.ap()[:, oci * 2 + ti, 0:512] if ti < 2
                                   else pp.ap()[:, 4, oci * 4:oci * 4 + 3])
                            i = t.matmul(out, wblk[:, bsel, kc, oci * 128:(oci + 1) * 128],
                                         xn[:, kc, t0:t0 + tn],
                                         start=(kc == 0), stop=(kc == NCH - 1))
                ph.sig(i, f"mm{pnm}{ocg}")
            @block.vector
            def _(v, ocg=ocg, pnm=pnm, ci=ci):
                ph.wait(v, f"mm{pnm}{ocg}")
                for oci in range(2):
                    oc = ocg * 2 + oci
                    rb = oc % 4
                    v.tensor_copy(raw[:, rb, 0:512], pp.ap()[:, oci * 2, 0:512])
                    v.tensor_copy(raw[:, rb, 512:1024], pp.ap()[:, oci * 2 + 1, 0:512])
                    v.tensor_copy(raw[:, rb, 1024:1027], pp.ap()[:, 4, oci * 4:oci * 4 + 3])
                    cb = oc % 2
                    v.tensor_scalar_mul(cacc[:, cb], raw[:, rb, 0:TOK], convc[:, ci, oc, 0:1])
                    for d in (1, 2, 3):
                        i = v.scalar_tensor_tensor(cacc[:, cb], raw[:, rb, d:d + TOK],
                                                   convc[:, ci, oc, d:d + 1], cacc[:, cb],
                                                   AL.mult, AL.add)
                    ph.sig(i, f"ca{pnm}{oc}")
                    if oci == 1:
                        ph.also(f"cv{pnm}{ocg}", f"ca{pnm}{oc}")
            @block.scalar
            def _(sc, ocg=ocg, pnm=pnm, dst=dst):
                for oci in range(2):
                    oc = ocg * 2 + oci
                    ph.wait(sc, f"ca{pnm}{oc}")
                    ph.sig(sc.activation(dst[:, oc, :], cacc[:, oc % 2], ACT.Silu),
                           f"sl{pnm}{oc}")
        @block.sync
        def _(sy):
            for oc in range(NCH):
                ph.wait(sy, f"slv{oc}")
            ph.sigd(sy.dma_start(out=fm(o_v), in_=vt[:]), "stv")
        if "kraw" in dbg:
            @block.gpsimd
            def _(gp):
                for oc in range(NCH):
                    ph.wait(gp, f"slk{oc}")
                ph.sigd(gp.dma_start(out=fm(dbg["kraw"]), in_=kt[:]), "dkr")
                ph.wait(gp, "dkr")


def emit_stats(nc, block, ph, pp, qt, kt, vt, wsv, onecb, onerf, onerb, onerbs,
               mlw1, mlb1, mlw2, mlb2, dtb, negeA, o_beta, o_g, o_kn, o_qs, dbg):
    import os
    NHL = int(os.environ.get('NHL', NH))
    CUT = int(os.environ.get('SCUT', 99))
    """Per head h. Row map (tensor@base, bases limited to {0,32,64} for MM use):
      S1:  ssqvh@0  ssqe@32  ssqq@64     (wave A)
      S2:  ssqv@0   sl1@32   ssqk@64     (wave B)
      BR:  dot@0 (wave C); later beta@0, amp@64
      R1  = sqrt(S1+1e-6): sqvh@0 sl2@32 sqq@64
      R2a = sqrt(S2+1e-6): sqv@0  .      sqk@64
      R2b = sqrt(S2):      .     .       nrmk@64
      CT:  cos chain @0 -> cos@0 ; later rktrue/rkamp @64
      RQK = recip(R1): rq@64 ; S1 = recip(R2a): rk@64
      RB[0]=bf16(rkamp) RB[1]=bf16(rq-ish) RB[2]=bf16(rk-ish)
    """
    with (
        nc.sbuf_tensor("vhp", [128, TOK], BF16) as vhp,
        nc.sbuf_tensor("ep", [128, TOK], BF16) as ep,
        nc.sbuf_tensor("pr", [128, 2, TOK], BF16) as pr,
        nc.sbuf_tensor("gsc", [128, 2, TOK], F32) as gsc,
        nc.sbuf_tensor("RB", [128, 3, TOK], BF16) as RB,
        nc.sbuf_tensor("S1", [128, TOK], F32) as S1,
        nc.sbuf_tensor("S2", [128, TOK], F32) as S2,
        nc.sbuf_tensor("R1", [128, TOK], F32) as R1,
        nc.sbuf_tensor("R2a", [128, TOK], F32) as R2a,
        nc.sbuf_tensor("R2b", [128, TOK], F32) as R2b,
        nc.sbuf_tensor("RQK", [128, TOK], F32) as RQK,
        nc.sbuf_tensor("CT", [128, TOK], F32) as CT,
        nc.sbuf_tensor("BR", [128, TOK], F32) as BR,
        nc.sbuf_tensor("hid", [32, 2, TOK], F32) as hid,
        nc.sbuf_tensor("M", [128, TOK], BF16) as M,
    ):
        def colsum(t, src_ap, base, start):
            t.matmul(pp.ap()[base:base + 1, 2, :], onecb[:], src_ap[:, 0:512],
                     start=True, stop=True, tile_position=(0, base))
            return t.matmul(pp.ap()[base:base + 1, 3, :], onecb[:], src_ap[:, 512:1024],
                            start=True, stop=True, tile_position=(0, base))

        @block.vector
        def _(v):
            ph.sig(v.memset(M[:], 0.0), "Mz")
        for h in range(NHL):
            tg = f"h{h}"
            @block.tensor
            def _(t, h=h, tg=tg):
                if h == 0:
                    ph.wait(t, "ld0")
                    for oc in range(NCH):
                        ph.wait(t, f"slk{oc}", f"slv{oc}", f"slq{oc}")
                else:
                    ph.wait(t, f"h{h-1}gx", f"h{h-1}ev")
                t.ldweights(wsv[:])
                for ti in range(2):
                    i = t.matmul(pp.ap()[:, ti, :], wsv[:], kt[:, h, ti * 512:(ti + 1) * 512],
                                 start=True, stop=True)
                ph.sig(i, tg + "vh")
            @block.vector
            def _(v, h=h, tg=tg):
                ph.wait(v, tg + "vh")
                v.tensor_copy(vhp[:, 0:512], pp.ap()[:, 0, :])
                v.tensor_copy(vhp[:, 512:1024], pp.ap()[:, 1, :])
                v.tensor_sub(ep[:, 0:512], pp.ap()[:, 0, :], vt[:, h, 0:512])
                ph.sig(v.tensor_sub(ep[:, 512:1024], pp.ap()[:, 1, :], vt[:, h, 512:1024]),
                       tg + "e")
            if CUT < 2:
                continue
            @block.vector
            def _(v, h=h, tg=tg):
                ph.wait(v, tg + "e")
                if h > 0:
                    ph.wait(v, f"h{h-1}S2c")
                ph.sig(v.tensor_mul(pr[:, 0], vhp[:], vhp[:]), tg + "pA0")
                ph.sig(v.tensor_mul(pr[:, 1], ep[:], ep[:]), tg + "pA1")
            @block.tensor
            def _(t, h=h, tg=tg):
                ph.wait(t, tg + "pA0")
                colsum(t, pr[:, 0], 0, True)
                ph.wait(t, tg + "pA1")
                ph.sig(colsum(t, pr[:, 1], 32, False), tg + "sA0")
            @block.vector
            def _(v, h=h, tg=tg):
                ph.wait(v, tg + "sA0")
                ph.sig(v.tensor_mul(pr[:, 0], qt[:, h, :], qt[:, h, :]), tg + "pA2")
            @block.tensor
            def _(t, h=h, tg=tg):
                ph.wait(t, tg + "pA2")
                ph.sig(colsum(t, pr[:, 0], 64, False), tg + "sA")
            @block.vector
            def _(v, tg=tg):
                ph.wait(v, tg + "sA")
                v.tensor_copy(S1[:, 0:512], pp.ap()[:, 2, :])
                ph.sig(v.tensor_copy(S1[:, 512:1024], pp.ap()[:, 3, :]), tg + "S1c")
            if CUT < 3:
                continue
            @block.vector
            def _(v, h=h, tg=tg):
                ph.wait(v, tg + "S1c")
                ph.sig(v.tensor_mul(pr[:, 0], vt[:, h, :], vt[:, h, :]), tg + "pB0")
            @block.scalar
            def _(sc, tg=tg):
                ph.wait(sc, tg + "e", tg + "S1c")
                ph.sig(sc.activation(pr[:, 1], ep[:], ACT.Abs), tg + "pB1")
            @block.tensor
            def _(t, h=h, tg=tg):
                ph.wait(t, tg + "pB0")
                colsum(t, pr[:, 0], 0, True)
                ph.wait(t, tg + "pB1")
                ph.sig(colsum(t, pr[:, 1], 64, False), tg + "sB0")
            @block.vector
            def _(v, h=h, tg=tg):
                ph.wait(v, tg + "sB0")
                ph.sig(v.tensor_mul(pr[:, 0], kt[:, h, :], kt[:, h, :]), tg + "pB2")
            @block.tensor
            def _(t, h=h, tg=tg):
                ph.wait(t, tg + "pB2")
                ph.sig(colsum(t, pr[:, 0], 32, False), tg + "sB")
            @block.vector
            def _(v, h=h, tg=tg):
                ph.wait(v, tg + "sB")
                v.tensor_copy(S2[:, 0:512], pp.ap()[:, 2, :])
                ph.sig(v.tensor_copy(S2[:, 512:1024], pp.ap()[:, 3, :]), tg + "S2c")
                if h == 0:
                    ph.wait(v, "Mz")
                v.tensor_copy(M[64:65, 0:512], pp.ap()[64:65, 2, :])
                ph.sig(v.tensor_copy(M[64:65, 512:1024], pp.ap()[64:65, 3, :]), tg + "sl1M")
                ph.sig(v.tensor_mul(pr[:, 1], vhp[:], vt[:, h, :]), tg + "pC0")
            @block.tensor
            def _(t, h=h, tg=tg):
                ph.wait(t, tg + "pC0")
                ph.sig(colsum(t, pr[:, 1], 0, True), tg + "sC")
            @block.vector
            def _(v, h=h, tg=tg):
                ph.wait(v, tg + "sC")
                if h > 0:
                    ph.wait(v, "stbeta")
                v.tensor_copy(BR[:, 0:512], pp.ap()[:, 2, :])
                ph.sig(v.tensor_copy(BR[:, 512:1024], pp.ap()[:, 3, :]), tg + "S3c")
            if CUT < 4:
                continue
            @block.scalar
            def _(sc, tg=tg):
                ph.wait(sc, tg + "S2c")
                ph.sig(sc.activation(R1[:], S1[:], ACT.Sqrt, bias=1e-6), tg + "R1")
                ph.sig(sc.activation(R2a[:], S2[:], ACT.Sqrt, bias=1e-6), tg + "R2a")
                ph.sig(sc.activation(R2b[:], S2[:], ACT.Sqrt), tg + "R2b")
            @block.vector
            def _(v, h=h, tg=tg):
                if h == 0:
                    ph.wait(v, "Mz")
                ph.wait(v, tg + "R1", tg + "R2a", tg + "S3c")
                v.tensor_mul(CT[:], R1[:], R2a[:])
                v.tensor_scalar_add(CT[:], CT[:], 1e-6)
                v.reciprocal(CT[:], CT[:])
                v.tensor_mul(CT[:], BR[:], CT[:])
                ph.sig(v.tensor_scalar(M[0:1, :], CT[0:1, :], -1.0, 1.0, AL.mult, AL.add),
                       tg + "cos")
                v.tensor_copy(M[32:33, :], R1[32:33, :])
                v.tensor_copy(M[64:65, :], S2[64:65, :]) if False else None
                v.reciprocal(RQK[:], R1[:])
                ph.sig(v.reciprocal(S1[:], R2a[:]), tg + "rqk")
            if CUT < 5:
                continue
            @block.tensor
            def _(t, tg=tg):
                ph.wait(t, tg + "rqk", tg + "sl1M")
                for m, b0 in ((0, 6), (1, 0)):
                    for half in range(2):
                        s0 = half * 512
                        i = t.matmul(pp.ap()[0:32, b0 + half, :], mlw1[:, m, :],
                                     M[:, s0:s0 + 512], start=True, stop=True)
                    ph.sig(i, tg + f"mmh{m}")
            @block.scalar
            def _(sc, tg=tg):
                for m, b0 in ((0, 6), (1, 0)):
                    ph.wait(sc, tg + f"mmh{m}")
                    ph.sig(sc.activation(hid[:, m, :], pp.ap()[0:32, b0:b0 + 2, :],
                                         ACT.Silu, bias=mlb1[:, m:m + 1]), tg + f"hs{m}")
            @block.tensor
            def _(t, tg=tg):
                ph.wait(t, tg + "hs0", tg + "hs1")
                for m, bs in ((0, 0), (1, 32)):
                    for half in range(2):
                        i = t.matmul(pp.ap()[bs:bs + 1, 4 + half, :], mlw2[:, m:m + 1],
                                     hid[:, m, half * 512:(half + 1) * 512],
                                     start=True, stop=True, tile_position=(0, bs))
                ph.sig(i, tg + "mm2")
            @block.scalar
            def _(sc, tg=tg):
                ph.wait(sc, tg + "mm2")
                sc.activation(BR[0:1, 0:512], pp.ap()[0:1, 4, :], ACT.Sigmoid, bias=mlb2[0:1, 0:1])
                ph.sig(sc.activation(BR[0:1, 512:1024], pp.ap()[0:1, 5, :], ACT.Sigmoid,
                                     bias=mlb2[0:1, 0:1]), tg + "beta")
                sc.activation(BR[32:33, 0:512], pp.ap()[32:33, 4, :], ACT.Identity, bias=mlb2[32:33, 1:2])
                ph.sig(sc.activation(BR[32:33, 512:1024], pp.ap()[32:33, 5, :], ACT.Identity,
                                     bias=mlb2[32:33, 1:2]), tg + "amp")
            if CUT < 6:
                continue
            @block.sync
            def _(sy, h=h, tg=tg):
                ph.wait(sy, tg + "beta")
                ph.sigd(sy.dma_start(out=o_beta[h:h + 1, :], in_=BR[0:1, 0:TOK]), "stbeta")
                if h == int(os.environ.get("DH", 0)) and "M" in dbg:
                    ph.sigd(sy.dma_start(out=dbg["BR"], in_=BR[:]), "stbeta")
                    ph.sigd(sy.dma_start(out=dbg["hid"][0:32], in_=hid[:]), "stbeta")
            if CUT < 7:
                continue
            @block.vector
            def _(v, h=h, tg=tg):
                ph.wait(v, tg + "R2b", tg + "amp")
                v.tensor_scalar_max(CT[:], R2b[:], 1e-12)
                v.reciprocal(CT[:], CT[:])
                v.tensor_mul(CT[:], CT[:], BR[:])
                ph.sig(v.tensor_copy(RB[:, 0], CT[:]), tg + "rka")
            @block.tensor
            def _(t, tg=tg):
                ph.wait(t, tg + "rka")
                t.matmul(pp.ap()[:, 6, :], onerb[32:33, :], RB[32:33, 0, 0:512], start=True, stop=True)
                ph.sig(t.matmul(pp.ap()[:, 7, :], onerb[32:33, :], RB[32:33, 0, 512:1024],
                                start=True, stop=True), tg + "rkab")
            @block.vector
            def _(v, h=h, tg=tg):
                ph.wait(v, tg + "rkab")
                if h > 0:
                    ph.wait(v, f"h{h-1}gst")
                v.tensor_mul(gsc[:, 0, 0:512], kt[:, h, 0:512], pp.ap()[:, 6, :])
                ph.sig(v.tensor_mul(gsc[:, 0, 512:1024], kt[:, h, 512:1024], pp.ap()[:, 7, :]),
                       tg + "gx")
            @block.scalar
            def _(sc, h=h, tg=tg):
                ph.wait(sc, tg + "gx")
                sc.activation(gsc[:, 1], gsc[:, 0], ACT.Abs, bias=dtb[:, h:h + 1])
                sc.activation(gsc[:, 1], gsc[:, 1], ACT.Exp, scale=-1.0)
                sc.activation(gsc[:, 1], gsc[:, 1], ACT.Ln, bias=1.0)
                ph.sig(sc.activation(gsc[:, 0], gsc[:, 0], ACT.Relu, bias=dtb[:, h:h + 1]),
                       tg + "gr")
            @block.vector
            def _(v, h=h, tg=tg):
                ph.wait(v, tg + "gr")
                v.tensor_add(gsc[:, 0], gsc[:, 0], gsc[:, 1])
                ph.sig(v.tensor_scalar_mul(gsc[:, 0], gsc[:, 0], negeA[:, h:h + 1]),
                       tg + "g")
            @block.gpsimd
            def _(gp, h=h, tg=tg):
                if h == int(os.environ.get("DH", 0)) and "M" in dbg:
                    ph.wait(gp, tg + "beta")
                    ph.sigd(gp.dma_start(out=dbg["M"], in_=M[:]), "stg")
                ph.wait(gp, tg + "g")
                i = gp.dma_start(out=fm(o_g)[:, h, :], in_=gsc[:, 0])
                ph.sigd(i, "stg")
                ph.also(tg + "gst", "stg")
            if CUT < 8:
                continue
            @block.vector
            def _(v, tg=tg):
                ph.wait(v, tg + "rqk", tg + "gx")
                v.tensor_copy(RB[:, 1], RQK[:])
                ph.sig(v.tensor_copy(RB[:, 2], S1[:]), tg + "rqb")
            @block.tensor
            def _(t, tg=tg):
                ph.wait(t, tg + "rqb")
                t.matmul(pp.ap()[:, 0, :], onerbs[64:65, :], RB[64:65, 1, 0:512], start=True, stop=True)
                t.matmul(pp.ap()[:, 1, :], onerbs[64:65, :], RB[64:65, 1, 512:1024], start=True, stop=True)
                t.matmul(pp.ap()[:, 2, :], onerb[32:33, :], RB[32:33, 2, 0:512], start=True, stop=True)
                ph.sig(t.matmul(pp.ap()[:, 3, :], onerb[32:33, :], RB[32:33, 2, 512:1024],
                                start=True, stop=True), tg + "nbc")
            @block.vector
            def _(v, h=h, tg=tg):
                ph.wait(v, tg + "nbc")
                v.tensor_mul(pr[:, 0, 0:512], qt[:, h, 0:512], pp.ap()[:, 0, :])
                v.tensor_mul(pr[:, 0, 512:1024], qt[:, h, 512:1024], pp.ap()[:, 1, :])
                v.tensor_mul(pr[:, 1, 0:512], kt[:, h, 0:512], pp.ap()[:, 2, :])
                v.tensor_mul(pr[:, 1, 512:1024], kt[:, h, 512:1024], pp.ap()[:, 3, :])
                v.tensor_copy(qt[:, h, :], pr[:, 0])
                ph.sig(v.tensor_copy(kt[:, h, :], pr[:, 1]), tg + "nmul")
                ph.sig(v.engine_nop(), tg + "qsdone")
                ph.sig(v.engine_nop(), tg + "kndone")
                ph.sig(v.engine_nop(), tg + "ev")
        @block.sync
        def _(sy):
            if CUT >= 8:
                for h in range(NHL):
                    ph.wait(sy, f"h{h}qsdone", f"h{h}kndone")
                ph.sigd(sy.dma_start(out=fm(o_qs), in_=qt[:]), "stq")
                ph.sigd(sy.dma_start(out=fm(o_kn), in_=kt[:]), "stk")


def emit_gate(nc, block, ph, pp, xn, Wg1T, Wg2T, bg2c, o_sgate, wg, g1o, sg, bgc):
    if True:
        @block.gpsimd
        def _(gp):
            ph.wait(gp, "h15ev", "stg")
            ph.sigd(gp.dma_start(out=wg[:], in_=Wg1T.rearrange("(c p) o -> p c o", p=128)), "ldg1")
        @block.sync
        def _(sy):
            ph.sigd(sy.dma_start(out=bgc[:], in_=bg2c[:]), "ldbg")
        @block.tensor
        def _(t):
            ph.wait(t, "ldg1")
            for ti in range(2):
                for kc in range(NCH):
                    i = t.matmul(pp.ap()[:, ti, :], wg[:, kc, :],
                                 xn[:, kc, HALO + ti * 512:HALO + (ti + 1) * 512],
                                 start=(kc == 0), stop=(kc == NCH - 1))
            ph.sig(i, "g1mm")
        @block.vector
        def _(v):
            ph.wait(v, "g1mm")
            v.tensor_copy(g1o[:, 0:512], pp.ap()[:, 0, :])
            ph.sig(v.tensor_copy(g1o[:, 512:1024], pp.ap()[:, 1, :]), "g1e")
        @block.gpsimd
        def _(gp):
            ph.wait(gp, "g1e")
            ph.sigd(gp.dma_start(out=wg[:], in_=Wg2T.rearrange("(c p) o -> p c o", p=128)), "ldg2")
        for oc in range(NCH):
            @block.tensor
            def _(t, oc=oc):
                if oc == 0:
                    ph.wait(t, "ldg2", "ldbg")
                if oc >= 2:
                    ph.wait(t, f"sg{oc-2}")
                for ti in range(2):
                    i = t.matmul(pp.ap()[:, 2 + (oc % 2) * 2 + ti, :], wg[:, oc, :],
                                 g1o[:, ti * 512:(ti + 1) * 512], start=True, stop=True)
                ph.sig(i, f"g2mm{oc}")
            @block.scalar
            def _(sc, oc=oc):
                ph.wait(sc, f"g2mm{oc}")
                if oc >= 2:
                    ph.wait(sc, f"sgst{oc-2}")
                for ti in range(2):
                    i = sc.activation(sg[:, oc % 2, ti, :], pp.ap()[:, 2 + (oc % 2) * 2 + ti, :],
                                      ACT.Sigmoid, bias=bgc[:, oc:oc + 1])
                ph.sig(i, f"sg{oc}")
            @block.gpsimd
            def _(gp, oc=oc):
                ph.wait(gp, f"sg{oc}")
                i = gp.dma_start(out=fm(o_sgate)[:, oc, :],
                                 in_=sg[:, oc % 2].rearrange("p a t -> p (a t)"))
                ph.sigd(i, "stsg")
                ph.also(f"sgst{oc}", "stsg")


def emit_ffn(nc, block, ph, pp, xn, xT, o_acc, w1T, w2T, w3T, wq12s, wq3s, onecf, onerf, dbg):
    with (
        nc.sbuf_tensor("wf", [128, 2, FFH], F32) as wf,
        nc.sbuf_tensor("gac", [128, 4, FHC], F32) as gac,
        nc.sbuf_tensor("thr", [128, 8], F32) as thr,
        nc.sbuf_tensor("grow", [1, 16], F32) as grow,
        nc.sbuf_tensor("w1q", [128, 2, 2048], BF16) as w1q,
        nc.sbuf_tensor("w2q", [128, 2, 2048], BF16) as w2q,
        nc.sbuf_tensor("hb", [128, FHC, 512], BF16) as hb,
        nc.sbuf_tensor("sig", [128, 512], BF16) as sig,
        nc.sbuf_tensor("xr", [128, 8, 512], F32) as xr,
        nc.sbuf_tensor("accw", [128, 8, 512], F32) as accw,
        nc.sbuf_tensor("qscv", [128, 2, 2048], BF16) as qscv,
        nc.sbuf_tensor("qscg", [128, 2, 2048], BF16) as qscg,
    ):
        mats = [(w1T, 0, 16, FFH), (w2T, 1, 16, FFH), (w3T, 2, FHC, 2048)]
        seq = [(mi, c) for _, mi, ncc, _ in mats for c in range(ncc)]
        for n_, (mi, c) in enumerate(seq):
            W_, _, ncc, wdt = mats[mi]
            nm = f"f{mi}c{c}"
            parts = [(0, wdt)]
            @block.sync
            def _(sy, W_=W_, c=c, nm=nm, parts=parts, n_=n_):
                if n_ == 0:
                    ph.wait(sy, "h15ev", "stbeta")
                if n_ == 1:
                    ph.wait(sy, "stq", "stk")
                if n_ >= 2:
                    mi2, c2 = seq[n_ - 2]
                    ph.wait(sy, f"f{mi2}c{c2}r")
                p0, pn = parts[0]
                ph.sigd(sy.dma_start(
                    out=wf[:, c % 2, 0:pn],
                    in_=W_.rearrange("(c p) o -> p c o", p=128)[:, c, p0:p0 + pn]),
                    nm + "l0")
            def body0(e, mi=mi, c=c, nm=nm, parts=parts):
                ph.wait(e, nm + "l0")
                i = e.tensor_reduce(gac[:, mi, c:c + 1], wf[:, c % 2, 0:parts[0][1]],
                                    AX.X, AL.add, apply_absolute_value=True)
                ph.sig(i, nm + "r0")
            block.vector(body0)
            if len(parts) > 1:
                @block.sync
                def _(sy, W_=W_, c=c, nm=nm, parts=parts):
                    ph.wait(sy, nm + "r0")
                    p0, pn = parts[1]
                    ph.sigd(sy.dma_start(
                        out=wf[:, c % 2, 0:pn],
                        in_=W_.rearrange("(c p) o -> p c o", p=128)[:, c, p0:p0 + pn]),
                        nm + "l1")
                def body1(e, mi=mi, c=c, nm=nm, parts=parts):
                    ph.wait(e, nm + "l1")
                    e.tensor_reduce(gac[:, 3, 0:1], wf[:, c % 2, 0:parts[1][1]],
                                    AX.X, AL.add, apply_absolute_value=True)
                    i = e.tensor_add(gac[:, mi, c:c + 1], gac[:, mi, c:c + 1], gac[:, 3, 0:1])
                    ph.sig(i, nm + "r")
                block.vector(body1)
            else:
                def body1(e, mi=mi, c=c, nm=nm):
                    ph.wait(e, nm + "r0")
                    ph.sig(e.engine_nop(), nm + "r")
                block.vector(body1)
        @block.vector
        def _(v):
            for mi, c in seq:
                ph.wait(v, f"f{mi}c{c}r")
            for mi in range(3):
                ncc = 16 if mi < 2 else FHC
                i = v.tensor_reduce(gac[:, 3, mi:mi + 1], gac[:, mi, 0:ncc], AX.X, AL.add)
            ph.sig(i, "gp")
        @block.tensor
        def _(t):
            ph.wait(t, "gp", "g1e")
            ph.sig(t.matmul(pp.ap()[0:1, 0, 0:3], onecf[:], gac[:, 3, 0:3],
                            start=True, stop=True), "gs")
        @block.scalar
        def _(sc):
            ph.wait(sc, "gs")
            ph.sig(sc.activation(grow[0:1, 0:3], pp.ap()[0:1, 0, 0:3], ACT.Identity,
                                 scale=0.5 / (2048.0 * FFH), bias=0.5e-8), "gthr")
        @block.tensor
        def _(t):
            ph.wait(t, "gthr")
            ph.sig(t.matmul(pp.ap()[:, 1, 0:3], onerf[0:1, :], grow[0:1, 0:3],
                            start=True, stop=True), "gthb")
        @block.vector
        def _(v):
            ph.wait(v, "gthb")
            v.tensor_copy(thr[:, 0:3], pp.ap()[:, 1, 0:3])
            ph.sig(v.tensor_scalar_mul(thr[:, 4:7], pp.ap()[:, 1, 0:3], -1.0), "thrc")
        if "thr" in dbg:
            @block.gpsimd
            def _(gp):
                ph.wait(gp, "thrc")
                ph.sigd(gp.dma_start(out=dbg["thr"], in_=thr[:]), "dthr")
                if "gac" in dbg:
                    ph.sigd(gp.dma_start(out=dbg["gac"], in_=gac[:]), "dthr")

        def quant(e, qsc, dst, src, mi, n, tag):
            e.tensor_scalar(qsc[:, 0, 0:n], src, thr[:, mi:mi + 1], None, AL.is_gt)
            e.tensor_scalar(qsc[:, 1, 0:n], src, thr[:, 4 + mi:5 + mi], None, AL.is_lt)
            ph.sig(e.tensor_sub(dst, qsc[:, 0, 0:n], qsc[:, 1, 0:n]), tag)

        w1v = wq12s.rearrange("w (c p) o -> w p c o", p=128)
        w3v = wq3s.rearrange("(c p) o -> p c o", p=128)
        import os as _os
        for th in range(int(_os.environ.get('FFNTH', 2))):
            t0 = HALO + th * 512
            for hc in range(FHC):
                nm = f"t{th}h{hc}"
                if th == 0:
                    @block.sync
                    def _(sy, hc=hc, nm=nm):
                        if hc == 0:
                            ph.wait(sy, "thrc")
                        if hc >= 2:
                            ph.wait(sy, f"t0h{hc-2}q1", f"t0h{hc-2}q2")
                        ph.sigd(sy.dma_start(out=wf[:, hc % 2, 0:2048].rearrange("p (c o) -> p c o", c=16),
                                     in_=w1T.rearrange("(c p) o -> p c o", p=128)
                                     [:, :, hc * 128:(hc + 1) * 128]), nm + "l")
                        ph.sigd(sy.dma_start(out=wf[:, hc % 2, 2048:4096].rearrange("p (c o) -> p c o", c=16),
                                             in_=w2T.rearrange("(c p) o -> p c o", p=128)
                                             [:, :, hc * 128:(hc + 1) * 128]),
                                nm + "l")
                    @block.vector
                    def _(v, hc=hc, nm=nm):
                        ph.wait(v, nm + "l")
                        if hc >= 2:
                            ph.wait(v, f"t0h{hc-2}mm")
                            ph.wait(v, "stwq")
                        quant(v, qscv, w1q[:, hc % 2], wf[:, hc % 2, 0:2048], 0, 2048, nm + "q1")
                    @block.gpsimd
                    def _(gp, hc=hc, nm=nm):
                        ph.wait(gp, nm + "l")
                        if hc >= 2:
                            ph.wait(gp, f"t0h{hc-2}mm")
                        quant(gp, qscg, w2q[:, hc % 2], wf[:, hc % 2, 2048:4096], 1, 2048, nm + "q2")
                        ph.wait(gp, nm + "q1")
                        ph.sigd(gp.dma_start(out=w1v[0][:, :, hc * 128:(hc + 1) * 128],
                                     in_=w1q[:, hc % 2].rearrange("p (c o) -> p c o", c=16)), "stwq")
                        i = gp.dma_start(out=w1v[1][:, :, hc * 128:(hc + 1) * 128],
                                         in_=w2q[:, hc % 2].rearrange("p (c o) -> p c o", c=16))
                        ph.sigd(i, "stwq")
                else:
                    @block.sync
                    def _(sy, hc=hc, nm=nm):
                        if hc == 0:
                            ph.wait(sy, "stwq")
                        if hc < 2:
                            ph.wait(sy, f"t0w1h{FHC-1}mm")
                        if hc >= 2:
                            ph.wait(sy, f"t1h{hc-2}mm")
                        ph.sigd(sy.dma_start(out=w1q[:, hc % 2].rearrange("p (c o) -> p c o", c=16),
                                     in_=w1v[0][:, :, hc * 128:(hc + 1) * 128]), nm + "lq")
                        i = sy.dma_start(out=w2q[:, hc % 2].rearrange("p (c o) -> p c o", c=16),
                                         in_=w1v[1][:, :, hc * 128:(hc + 1) * 128])
                        ph.sigd(i, nm + "lq")
                @block.tensor
                def _(t, hc=hc, nm=nm, th=th, t0=t0):
                    if th == 0:
                        ph.wait(t, nm + "q1", nm + "q2")
                    else:
                        ph.wait(t, nm + "lq")
                        if hc == 0:
                            ph.wait(t, "accev01")
                    if hc >= 1:
                        ph.wait(t, f"t{th}h{hc-1}hid")
                    for kc in range(NCH):
                        t.matmul(pp.ap()[:, 0, :], w1q[:, hc % 2, kc * 128:(kc + 1) * 128],
                                 xn[:, kc, t0:t0 + 512], start=(kc == 0), stop=(kc == NCH - 1))
                        i = t.matmul(pp.ap()[:, 1, :], w2q[:, hc % 2, kc * 128:(kc + 1) * 128],
                                     xn[:, kc, t0:t0 + 512], start=(kc == 0), stop=(kc == NCH - 1))
                    ph.sig(i, nm + "mm")
                @block.scalar
                def _(sc, hc=hc, nm=nm, th=th):
                    ph.wait(sc, nm + "mm")
                    if hc >= 1:
                        ph.wait(sc, f"t{th}h{hc-1}hid")
                    ph.sig(sc.activation(sig[:], pp.ap()[:, 0, :], ACT.Sigmoid), nm + "sg")
                @block.vector
                def _(v, hc=hc, nm=nm):
                    ph.wait(v, nm + "sg")
                    ph.sig(v.tensor_mul(hb[:, hc, :], sig[:], pp.ap()[:, 1, :]), nm + "hid")
            if th == 0 and "hb" in dbg:
                @block.gpsimd
                def _(gp):
                    ph.wait(gp, f"t0h{FHC-1}hid")
                    ph.sigd(gp.dma_start(out=dbg["hb"], in_=hb[:].rearrange("p a o -> p (a o)")), "dhb")
            if th == 1 and "hb1" in dbg:
                @block.gpsimd
                def _(gp):
                    ph.wait(gp, f"t1h{FHC-1}hid")
                    ph.sigd(gp.dma_start(out=dbg["hb1"], in_=hb[:].rearrange("p a o -> p (a o)")), "dhb")
            if th == 1 and "wq1" in dbg:
                @block.gpsimd
                def _(gp):
                    ph.wait(gp, "stwq")
                    ph.sigd(gp.dma_start(out=dbg["wq1"][:, 0, :].rearrange("p (c o) -> p c o", c=16), in_=w1v[0][:, :, 2*128:3*128]), "dwq1")
                    ph.sigd(gp.dma_start(out=dbg["wq1"][:, 1, :].rearrange("p (c o) -> p c o", c=16), in_=w1v[1][:, :, 2*128:3*128]), "dwq1")
            for och in range(2):
                for hc in range(FHC):
                    nm = f"t{th}w{och}h{hc}"
                    fresh = (th == 0 and och == 0)
                    if fresh:
                        @block.sync
                        def _(sy, hc=hc, nm=nm):
                            if hc < 2:
                                ph.wait(sy, f"t0h{FHC-2+hc}q1", f"t0h{FHC-2+hc}q2")
                            if hc >= 2:
                                ph.wait(sy, f"t0w0h{hc-2}q")
                            ph.sigd(sy.dma_start(out=wf[:, hc % 2, 0:2048],
                                                 in_=w3T.rearrange("(c p) o -> p c o", p=128)[:, hc, :]),
                                    nm + "l")
                        @block.vector
                        def _(v, hc=hc, nm=nm):
                            ph.wait(v, nm + "l")
                            if hc < 2:
                                ph.wait(v, f"t0h{FHC-1}mm")
                            if hc >= 2:
                                ph.wait(v, f"t0w0h{hc-2}mm")
                                ph.wait(v, "stw3")
                            quant(v, qscv, w1q[:, hc % 2], wf[:, hc % 2, 0:2048], 2, 2048, nm + "q")
                        @block.gpsimd
                        def _(gp, hc=hc, nm=nm):
                            ph.wait(gp, nm + "q")
                            i = gp.dma_start(out=wq3s.rearrange("(c p) o -> p c o", p=128)[:, hc, :],
                                             in_=w1q[:, hc % 2])
                            ph.sigd(i, "stw3")
                    else:
                        @block.sync
                        def _(sy, hc=hc, nm=nm, och=och, th=th):
                            if hc == 0:
                                ph.wait(sy, "stw3")
                            if hc < 2:
                                if th == 1 and och == 0:
                                    ph.wait(sy, f"t1h{FHC-1}mm")
                                elif och == 1:
                                    ph.wait(sy, f"t{th}w0h{FHC-1}mm")
                            if hc >= 2:
                                ph.wait(sy, nm.replace(f"h{hc}", f"h{hc-2}") + "mm")
                            ph.sigd(sy.dma_start(
                                out=w1q[:, hc % 2, och * 1024:och * 1024 + 1024],
                                in_=w3v[:, hc, och * 1024:och * 1024 + 1024]), nm + "lq")
                    @block.tensor
                    def _(t, hc=hc, nm=nm, th=th, och=och, fresh=fresh):
                        ph.wait(t, nm + ("q" if fresh else "lq"))
                        if hc == 0:
                            ph.wait(t, f"t{th}h{FHC-1}hid")
                            if th == 1 or och == 1:
                                ph.wait(t, f"accst{th}{och-1}" if och == 1
                                        else f"accst{th-1}1")
                        for oc in range(8):
                            i = t.matmul(pp.ap()[:, oc, :],
                                         w1q[:, hc % 2, (och * 1024 if not fresh else 0) +
                                             oc * 128:(och * 1024 if not fresh else 0) + oc * 128 + 128],
                                         hb[:, hc, :], start=(hc == 0), stop=(hc == FHC - 1))
                        ph.sig(i, nm + "mm")
                        if hc == FHC - 1:
                            ph.also(f"accmm{th}{och}", nm + "mm")
                if th == 0 and och == 1 and "wq3" in dbg:
                    @block.gpsimd
                    def _(gp):
                        ph.wait(gp, "stw3")
                        ph.sigd(gp.dma_start(out=dbg["wq3"][:, 0, :], in_=w3v[:, 0, :]), "dwq3")
                        ph.sigd(gp.dma_start(out=dbg["wq3"][:, 1, :], in_=w3v[:, 7, :]), "dwq3")
                @block.sync
                def _(sy, th=th, och=och, t0=t0):
                    ph.sigd(sy.dma_start(out=xr[:],
                                         in_=fm(xT)[:, och * 8:(och + 1) * 8, t0:t0 + 512]),
                            f"xr{th}{och}")
                @block.vector
                def _(v, th=th, och=och):
                    ph.wait(v, f"accmm{th}{och}", f"xr{th}{och}")
                    for oc in range(8):
                        i = v.tensor_add(accw[:, oc, :], pp.ap()[:, oc, :], xr[:, oc, :])
                    ph.sig(i, f"accev{th}{och}")
                @block.gpsimd
                def _(gp, th=th, och=och):
                    ph.wait(gp, f"accev{th}{och}")
                    i = gp.dma_start(out=fm(o_acc)[:, och * 8:(och + 1) * 8, th * 512:th * 512 + 512],
                                     in_=accw[:])
                    ph.sigd(i, "stacc")
                    ph.also(f"accst{th}{och}", "stacc")




BNCH = 4
BT = 4096
C = 128
BNC = BT // C
LEVELS = [128, 64, 32, 16]


class P:
    def __init__(self, nc):
        self.nc = nc
        self.sems = {}
        self.tags = {}

    def _chan(self, inst, dma):
        eng = str(inst.ins.engine if hasattr(inst, "ins") else inst.engine)
        return ("D" if dma else "C") + eng

    def _ent(self, ch):
        if ch not in self.sems:
            self.sems[ch] = [self.nc.ctx.enter_context(self.nc.semaphore(ch[:28])), 0]
        return self.sems[ch]

    def sig(self, inst, name, n=1):
        ent = self._ent(self._chan(inst, False))
        ent[1] += n
        self.tags[name] = (self._chan(inst, False), ent[1])
        inst.then_inc(ent[0], n)
        return inst

    def sigd(self, inst, name):
        ch = self._chan(inst, True)
        ent = self._ent(ch)
        ent[1] += 16
        self.tags[name] = (ch, ent[1])
        inst.then_inc(ent[0], 16)
        return inst

    def also(self, name, other):
        self.tags[name] = self.tags[other]

    def wait(self, eng, *names):
        best = {}
        for name in names:
            if name in self.tags:
                ch, val = self.tags[name]
                best[ch] = max(best.get(ch, 0), val)
            else:
                self.missing = getattr(self, "missing", set())
                self.missing.add(name)
        for ch, val in best.items():
            eng.wait_ge(self.sems[ch][0], val)


def _consts():
    upm = np.zeros((C, C), np.float32)
    for s in range(C):
        upm[s, s + 1:] = 1.0
    bds = np.zeros((C, 16, C), np.float32)
    for d in range(8):
        for t in range(d, C):
            if d == 0 or (t % 8) >= d:
                if d >= 1:
                    bds[t - d, d, t] = 1.0
                bds[t - d, 8 + d, t] = 1.0
    bmb = np.zeros((C, C), BFD)
    bmo = np.zeros((C, C), BFD)
    for p in range(8):
        lo = 16 * p
        bmb[:, lo:lo + 8] = 1.0
        if p % 2 == 1:
            bmo[:, lo:lo + 8] = 1.0
    return upm, bds, bmb, bmo


def stp0(apx, col, reps):
    return bass.AP(tensor=apx.tensor, offset=apx.offset + col,
                   ap=[list(apx.ap[0]), [0, reps]])


def anchor_ap(apx, lvl, half):
    return bass.AP(tensor=apx.tensor, offset=apx.offset + (half - 1),
                   ap=[list(apx.ap[0]), [lvl, C // lvl], [0, half]])


def build_launch_B(nci=BNC, debug=()):
    nc = bass.Bass("TRN2", target_bir_lowering=False, debug=False)
    DT = nc.dram_tensor
    ph = P(nc)
    Tn = lambda ci, n: f"{ci}:{n}"

    i_k = DT("i_k", [BNCH, 128, BT], BF16, kind="ExternalInput").ap()
    i_q = DT("i_q", [BNCH, 128, BT], BF16, kind="ExternalInput").ap()
    i_v = DT("i_v", [BNCH, 128, BT], BF16, kind="ExternalInput").ap()
    i_g = DT("i_g", [BNCH, 128, BT], F32, kind="ExternalInput").ap()
    i_b = DT("i_b", [128, BNCH, BNC], F32, kind="ExternalInput").ap()
    i_sg = DT("i_sg", [BNCH, 128, BNC * 128], BF16, kind="ExternalInput").ap()
    i_nw = DT("i_nw", [128, 128], F32, kind="ExternalInput").ap()
    o_o = DT("o_o", [BNC, 128, BNCH * 128], BF16, kind="ExternalOutput").ap()
    dbg = {}
    for name in debug:
        dt_ = BF16 if name in ("Rt", "U") else F32
        dbg[name] = DT("dbg_" + name, [nci, 128, C], dt_, kind="ExternalOutput").ap()

    upm_a, bds_a, bmb_a, bmo_a = _consts()
    c_upm = nc.inline_tensor(upm_a, name="c_upm").ap()
    c_bds = nc.inline_tensor(bds_a, name="c_bds").ap()
    c_bmb = nc.inline_tensor(bmb_a, name="c_bmb").ap()
    c_bmo = nc.inline_tensor(bmo_a, name="c_bmo").ap()
    c_id = nc.inline_tensor(np.eye(C, dtype=BFD), name="c_id").ap()
    c_oneb = nc.inline_tensor(np.ones((128, 1), BFD), name="c_oneb").ap()
    CVALS = [0.0, 1.0, -1.0, 1e-5, 1.0 / 128, -0.5]
    c_cvs = nc.inline_tensor(np.tile(np.array(CVALS, np.float32), (128, 1)),
                             name="c_cvs").ap()

    with contextlib.ExitStack() as es:
        E = es.enter_context
        tn = {}
        def S(name, shape, dt):
            tn[name] = E(nc.sbuf_tensor(name, shape, dt))
            return tn[name]
        kc = S("kc", [128, BNCH, BT], BF16)
        qc = S("qc", [128, BNCH, BT], BF16)
        vc = S("vc", [128, BNCH, BT], BF16)
        gbuf = S("gbuf", [128, 2, BNCH, C], F32)
        sgbuf = S("sgbuf", [128, 2, BNCH, C], BF16)
        obuf = S("obuf", [128, 2, BNCH * 128], BF16)
        bcs = S("bcs", [128, BNCH, BNC], F32)
        nw = S("nw", [128, 128], F32)
        upm = S("upm", [128, C], F32)
        bdm = S("bdm", [128, 16, C], F32)
        bmb = S("bmb", [128, C], BF16)
        bmo = S("bmo", [128, C], BF16)
        ident = S("ident", [128, C], BF16)
        onecb = S("onecb", [128, 1], BF16)
        cvs = S("cvs", [128, len(CVALS)], F32)
        zt = S("zt", [128, C], F32)
        L4 = S("L4", [128, 2, BNCH, C], F32)
        EP = S("EP", [128, BNCH, 2, C], F32)
        EXPP = S("EXPP", [128, 2, BNCH, 2, C], BF16)
        E1f = S("E1f", [128, 2, BNCH, C], F32)
        ELcf = S("ELcf", [128, 2, BNCH, 1], F32)
        ARGP = S("ARGP", [128, BNCH, 4, C], F32)
        F4 = S("F4", [128, 2, BNCH, 4, C], BF16)
        KF4 = S("KF4", [128, BNCH, 4, C], BF16)
        QF4 = S("QF4", [128, BNCH, 4, C], BF16)
        KFB = S("KFB", [128, BNCH, C], BF16)
        KFO = S("KFO", [128, BNCH, C], BF16)
        kap4 = S("kap4", [128, 2, BNCH, C], BF16)
        qt4 = S("qt4", [128, 2, BNCH, C], BF16)
        kb4 = S("kb4", [128, 2, BNCH, C], BF16)
        Wf = S("Wf", [128, 2, BNCH, C], F32)
        PBK = S("PBK", [128, 4, 8, C], BF16)
        PBQ = S("PBQ", [128, 4, 8, C], BF16)
        bandc = S("bandc", [128, BNCH, 16], F32)
        CkT = S("CkT", [128, BNCH, C], F32)
        CqT = S("CqT", [128, BNCH, C], F32)
        CqTb = S("CqTb", [128, BNCH, C], BF16)
        MT4 = S("MT4", [128, 2, BNCH, C], BF16)
        MU4 = S("MU4", [128, BNCH, C], BF16)
        PT4 = S("PT4", [128, 6, BNCH, C], BF16)
        Rt4 = S("Rt4", [128, BNCH, C], BF16)
        Ut4 = S("Ut4", [128, BNCH, C], BF16)
        VT4 = S("VT4", [128, BNCH, C], BF16)
        KBT4 = S("KBT4", [128, BNCH, C], BF16)
        Sf4 = S("Sf4", [128, BNCH, C], F32)
        Sb4 = S("Sb4", [128, BNCH, C], BF16)
        Osb = S("Osb", [128, BNCH, C], F32)
        TRS = S("TRS", [128, C], F32)
        rms = S("rms", [128, BNCH, 4], F32)
        pf = E(nc.psum_tensor("pf", [128, 7, 512], F32))
        ptb = E(nc.psum_tensor("ptb", [128, 1, 1024], BF16))
        block = E(nc.Block())

        for vi, val in enumerate(CVALS):
            nc.const_aps.register(F32, float(val), cvs[:, vi:vi + 1])

        ns = dict(tn)
        ns.update(pf=pf, ptb=ptb, i_g=i_g, i_sg=i_sg, o_o=o_o)

        @block.sync
        def _(sy):
            for dst, src in ((upm, c_upm), (bdm, c_bds), (bmb, c_bmb), (bmo, c_bmo),
                             (ident, c_id), (onecb, c_oneb), (cvs, c_cvs),
                             (bcs, i_b), (nw, i_nw)):
                ph.sigd(sy.dma_start(out=dst[:], in_=src[:]), "ld0")
            for ch in range(BNCH):
                ph.sigd(sy.dma_start(out=kc[:, ch, :], in_=i_k[ch]), "ld0")
                ph.sigd(sy.dma_start(out=vc[:, ch, :], in_=i_v[ch]), "ld0")

        @block.gpsimd
        def _(gp):
            for ch in range(BNCH):
                ph.sigd(gp.dma_start(out=qc[:, ch, :], in_=i_q[ch]), "ld0g")

        @block.vector
        def _(v):
            ph.wait(v, "ld0", "ld0g")
            v.memset(zt[:], 0.0)
            v.memset(Sf4[:], 0.0)
            v.memset(Sb4[:], 0.0)
            v.memset(pf.ap()[:, 0, :], 0.0)
            v.memset(pf.ap()[:, 1, :], 0.0)
            ph.sig(v.memset(pf.ap()[:, 4, :], 0.0), "init")

        for eng in (block.scalar, block.tensor, block.gpsimd, block.sync):
            def _p(e):
                ph.wait(e, "init")
            eng(_p)

        for ci in range(nci):
            emit_early(block, ph, Tn, ci, ns, dbg)
            if ci > 0:
                emit_rest(block, ph, Tn, ci - 1, ns, dbg)
        emit_rest(block, ph, Tn, nci - 1, ns, dbg)

        @block.sync
        def _(sy):
            ph.wait(sy, Tn(nci - 1, "ost"))
        @block.gpsimd
        def _(gp):
            ph.wait(gp, Tn(nci - 1, "ost"))
    nc._ph = ph
    return nc


def emit_early(block, ph, Tn, ci, ns, dbg):
    import os
    STOP = float(os.environ.get("STOPB", "99"))
    g = type("NS", (), ns)
    s2 = ci % 2
    t0 = ci * C
    R = range(BNCH)
    Ksl = lambda ch: g.kc[:, ch, t0:t0 + C]
    Qsl = lambda ch: g.qc[:, ch, t0:t0 + C]
    Vsl = lambda ch: g.vc[:, ch, t0:t0 + C]
    Gsl = lambda ch: g.gbuf[:, s2, ch, :]
    Lsl = lambda ch: g.L4[:, s2, ch, :]
    ELs = lambda ch: g.EXPP[:, s2, ch, 0, :]
    ESUFs = lambda ch: g.EXPP[:, s2, ch, 1, :]
    kaps = lambda ch: g.kap4[:, s2, ch, :]
    qts = lambda ch: g.qt4[:, s2, ch, :]
    kbs = lambda ch: g.kb4[:, s2, ch, :]
    MTs = lambda ch: g.MT4[:, s2, ch, :]
    pCk = lambda ch, p0=0, p1=128, c0=0, c1=C: g.pf.ap()[p0:p1, 0, ch * 128 + c0:ch * 128 + c1]
    pCq = lambda ch, p0=0, p1=128, c0=0, c1=C: g.pf.ap()[p0:p1, 1, ch * 128 + c0:ch * 128 + c1]
    pR = lambda ch: g.pf.ap()[:, 2, ch * 128:ch * 128 + C]
    pSq = lambda ch: g.pf.ap()[:, 3, ch * 128:ch * 128 + C]
    pSq2 = lambda ch: g.pf.ap()[:, 4, ch * 128:ch * 128 + C]
    pBd = lambda ch, p0, p1, c0, c1: g.pf.ap()[p0:p1, 4, ch * 128 + c0:ch * 128 + c1]
    pAp = lambda ch: g.pf.ap()[:, 5, ch * 128:ch * 128 + C]
    pO = lambda ch: g.pf.ap()[:, 6, ch * 128:ch * 128 + C]
    pT = lambda ch: g.ptb.ap()[:, 0, ch * 128:ch * 128 + C]

    @block.sync
    def _(sy):
        ph.wait(sy, *[Tn(ci - 2, f"scan{ch}") for ch in R],
                *[Tn(ci - 2, f"exps{ch}") for ch in R],
                *[Tn(ci - 2, f"epi{ch}") for ch in R])
        src = g.i_g[:, :, t0:t0 + C].rearrange("c p t -> p c t")
        ph.sigd(sy.dma_start(out=g.gbuf[:, s2, :, :], in_=src), Tn(ci, "gld"))
        src = g.i_sg[:, :, t0:t0 + C].rearrange("c p t -> p c t")
        ph.sigd(sy.dma_start(out=g.sgbuf[:, s2, :, :], in_=src), Tn(ci, "sgld"))

    if STOP < 3:
        return
    @block.vector
    def _(v):
        for ch in R:
            ph.wait(v, Tn(ci, "gld"), Tn(ci - 2, "dbgL"))
            ph.sig(v.tensor_tensor_scan(Lsl(ch), Gsl(ch), g.zt[:], 0.0,
                                        AL.add, AL.add), Tn(ci, f"scan{ch}"))
        for ch in R:
            ph.wait(v, Tn(ci - 1, f"exps{ch}"))
            v.tensor_copy(g.EP[:, ch, 0, :], Lsl(ch))
            v.tensor_tensor(g.EP[:, ch, 1, :], stp0(Lsl(ch), C - 1, C), Lsl(ch),
                            AL.subtract)
            for li, lvl in enumerate(LEVELS):
                half = lvl // 2
                Av = g.ARGP[:, ch, li, :].rearrange("p (a b) -> p a b", b=lvl)
                Lv = Lsl(ch).rearrange("p (a b) -> p a b", b=lvl)
                v.tensor_tensor(Av[:, :, 0:half], anchor_ap(Lsl(ch), lvl, half),
                                Lv[:, :, 0:half], AL.subtract)
                i = v.tensor_tensor(Av[:, :, half:lvl], Lv[:, :, half:lvl],
                                    anchor_ap(Lsl(ch), lvl, half), AL.subtract)
            ph.sig(i, Tn(ci, f"args{ch}"))

    if STOP < 4:
        return
    @block.scalar
    def _(sc):
        for ch in R:
            ph.wait(sc, Tn(ci, f"args{ch}"), Tn(ci - 2, f"kqb{ch}"),
                    Tn(ci - 2, f"kfq{ch}"), Tn(ci - 2, f"wfe{ch}"),
                    Tn(ci - 2, f"sfu{ch}"))
            sc.activation(g.EXPP[:, s2, ch, :, :], g.EP[:, ch, :, :], ACT.Exp)
            sc.activation(g.E1f[:, s2, ch, :], Gsl(ch), ACT.Exp)
            sc.activation(g.ELcf[:, s2, ch, 0:1], Lsl(ch)[:, C - 1:C], ACT.Exp)
            ph.sig(sc.activation(g.F4[:, s2, ch, :, :], g.ARGP[:, ch, :, :], ACT.Exp),
                   Tn(ci, f"exps{ch}"))



def emit_rest(block, ph, Tn, ci, ns, dbg):
    import os
    STOP = float(os.environ.get("STOPB", "99"))
    g = type("NS", (), ns)
    s2 = ci % 2
    t0 = ci * C
    R = range(BNCH)
    Ksl = lambda ch: g.kc[:, ch, t0:t0 + C]
    Qsl = lambda ch: g.qc[:, ch, t0:t0 + C]
    Vsl = lambda ch: g.vc[:, ch, t0:t0 + C]
    Gsl = lambda ch: g.gbuf[:, s2, ch, :]
    Lsl = lambda ch: g.L4[:, s2, ch, :]
    ELs = lambda ch: g.EXPP[:, s2, ch, 0, :]
    ESUFs = lambda ch: g.EXPP[:, s2, ch, 1, :]
    kaps = lambda ch: g.kap4[:, s2, ch, :]
    qts = lambda ch: g.qt4[:, s2, ch, :]
    kbs = lambda ch: g.kb4[:, s2, ch, :]
    MTs = lambda ch: g.MT4[:, s2, ch, :]
    pCk = lambda ch, p0=0, p1=128, c0=0, c1=C: g.pf.ap()[p0:p1, 0, ch * 128 + c0:ch * 128 + c1]
    pCq = lambda ch, p0=0, p1=128, c0=0, c1=C: g.pf.ap()[p0:p1, 1, ch * 128 + c0:ch * 128 + c1]
    pR = lambda ch: g.pf.ap()[:, 2, ch * 128:ch * 128 + C]
    pSq = lambda ch: g.pf.ap()[:, 3, ch * 128:ch * 128 + C]
    pSq2 = lambda ch: g.pf.ap()[:, 4, ch * 128:ch * 128 + C]
    pBd = lambda ch, p0, p1, c0, c1: g.pf.ap()[p0:p1, 4, ch * 128 + c0:ch * 128 + c1]
    pAp = lambda ch: g.pf.ap()[:, 5, ch * 128:ch * 128 + C]
    pO = lambda ch: g.pf.ap()[:, 6, ch * 128:ch * 128 + C]
    pT = lambda ch: g.ptb.ap()[:, 0, ch * 128:ch * 128 + C]

    if STOP < 4.5:
        return
    @block.gpsimd
    def _(gp):
        for ch in R:
            sl = ch
            ph.wait(gp, Tn(ci, f"exps{ch}"), Tn(ci - 1, f"bmm{ch}"))
            gp.tensor_mul(g.PBQ[:, sl, 0, :], Qsl(ch), Ksl(ch))
            for d in range(1, 8):
                wsl = d % 2
                wsrc = Ksl(ch)[:, 0:C - 1] if d == 1 else g.Wf[:, 1 - wsl, ch, 0:C - 1]
                gp.tensor_mul(g.Wf[:, wsl, ch, 1:C], wsrc, g.E1f[:, s2, ch, 1:C])
                if d == 7:
                    ph.sig(gp.engine_nop(), Tn(ci, f"wfe{ch}"))
                gp.tensor_mul(g.PBK[:, sl, d, d:C], Ksl(ch)[:, d:C],
                              g.Wf[:, wsl, ch, d:C])
                i = gp.tensor_mul(g.PBQ[:, sl, d, d:C], Qsl(ch)[:, d:C],
                                  g.Wf[:, wsl, ch, d:C])
            ph.sig(i, Tn(ci, f"pbs{ch}"))

    if STOP < 5:
        return
    @block.vector
    def _(v):
        for ch in R:
            ph.wait(v, Tn(ci, f"exps{ch}"), Tn(ci - 2, f"omm{ch}"),
                    Tn(ci - 2, f"ktr{ch}"))
            v.tensor_mul(kaps(ch), Ksl(ch), ELs(ch))
            v.tensor_mul(qts(ch), Qsl(ch), ELs(ch))
            ph.sig(v.tensor_mul(kbs(ch), Ksl(ch), ESUFs(ch)), Tn(ci, f"kqb{ch}"))
        for ch in R:
            ph.wait(v, Tn(ci - 1, f"lvl{ch}"))
            for li in range(4):
                v.tensor_mul(g.KF4[:, ch, li, :], Ksl(ch), g.F4[:, s2, ch, li, :])
                v.tensor_mul(g.QF4[:, ch, li, :], Qsl(ch), g.F4[:, s2, ch, li, :])
            v.tensor_mul(g.KFB[:, ch, :], g.KF4[:, ch, 3, :], g.bmb[:])
            ph.sig(v.tensor_mul(g.KFO[:, ch, :], g.KF4[:, ch, 3, :], g.bmo[:]),
                   Tn(ci, f"kfq{ch}"))

    if STOP < 6:
        return
    @block.tensor
    def _(t):
        for ch in R:
            ph.wait(t, Tn(ci, f"kfq{ch}"), Tn(ci - 1, "msk3"))
            for w in range(4):
                b0 = 32 * w
                for lhs, a0 in ((g.KFB, b0 + 8), (g.KFO, b0 + 24)):
                    t.matmul(pCk(ch, b0, b0 + 32, a0, a0 + 8), lhs[:, ch, b0:b0 + 32],
                             g.KF4[:, ch, 3, a0:a0 + 8], start=True, stop=True,
                             tile_position=(0, b0))
                    t.matmul(pCq(ch, b0, b0 + 32, a0, a0 + 8), lhs[:, ch, b0:b0 + 32],
                             g.QF4[:, ch, 3, a0:a0 + 8], start=True, stop=True,
                             tile_position=(0, b0))
            for li, lvl in ((2, 32), (1, 64), (0, 128)):
                half = lvl // 2
                for p0 in range(0, C, lvl):
                    lhs = g.KF4[:, ch, li, p0:p0 + half]
                    t.matmul(pCk(ch, p0, p0 + half, p0 + half, p0 + lvl), lhs,
                             g.KF4[:, ch, li, p0 + half:p0 + lvl],
                             start=True, stop=True, tile_position=(0, p0))
                    i = t.matmul(pCq(ch, p0, p0 + half, p0 + half, p0 + lvl), lhs,
                                 g.QF4[:, ch, li, p0 + half:p0 + lvl],
                                 start=True, stop=True, tile_position=(0, p0))
            ph.sig(i, Tn(ci, f"lvl{ch}"))
        for ch in R:
            sl = ch
            ph.wait(t, Tn(ci, f"pbs{ch}"), Tn(ci - 1, "mu53"),
                    Tn(ci - 1, "bndc3"))
            for d in range(1, 8):
                t.matmul(pBd(ch, 0, 128 - d, d, d + 1), g.PBK[:, sl, d, d:C],
                         g.onecb[:], start=True, stop=True, tile_position=(0, 0))
            for d in range(8):
                i = t.matmul(pBd(ch, 0, 128 - d if d else 128, 8 + d, 9 + d),
                             g.PBQ[:, sl, d, d:C], g.onecb[:],
                             start=True, stop=True, tile_position=(0, 0))
            ph.sig(i, Tn(ci, f"bmm{ch}"))

    if STOP < 7:
        return
    @block.vector
    def _(v):
        for ch in R:
            ph.wait(v, Tn(ci, "bmm3"))
            ph.sig(v.tensor_copy(g.bandc[:, ch, 1:16], pBd(ch, 0, 128, 1, 16)),
                   Tn(ci, f"bndc{ch}"))
            ph.wait(v, Tn(ci, "lvl3"), Tn(ci - 1, "dbgCkT"), Tn(ci - 1, "dbgCqT"))
            v.tensor_mul(g.CkT[:, ch, :], pCk(ch), g.upm[:])
            ph.sig(v.tensor_mul(g.CqT[:, ch, :], pCq(ch), g.upm[:]),
                   Tn(ci, f"msk{ch}"))
            for d in range(1, 8):
                v.scalar_tensor_tensor(g.CkT[:, ch, :], g.bdm[:, d, :],
                                       g.bandc[:, ch, d:d + 1], g.CkT[:, ch, :],
                                       AL.mult, AL.add)
            for d in range(8):
                v.scalar_tensor_tensor(g.CqT[:, ch, :], g.bdm[:, 8 + d, :],
                                       g.bandc[:, ch, 8 + d:9 + d], g.CqT[:, ch, :],
                                       AL.mult, AL.add)
            ph.wait(v, Tn(ci - 2, f"apm0{ch}"))
            ph.sig(v.tensor_scalar_mul(MTs(ch), g.CkT[:, ch, :],
                                       g.bcs[:, ch, ci:ci + 1]), Tn(ci, f"MT{ch}"))
            ph.wait(v, Tn(ci - 1, f"omm{ch}"))
            ph.sig(v.tensor_copy(g.CqTb[:, ch, :], g.CqT[:, ch, :]),
                   Tn(ci, f"cqb{ch}"))

    if STOP < 8:
        return
    @block.tensor
    def _(t):
        for ch in R:
            ph.wait(t, Tn(ci, f"MT{ch}"), Tn(ci - 1, "kbte3"))
            ph.sig(t.transpose(pT(ch), MTs(ch), g.ident[:]), Tn(ci, f"mtr{ch}"))
        for ch in R:
            ph.wait(t, Tn(ci, f"kqb{ch}"), Tn(ci - 1, f"sb{ch}"),
                    Tn(ci - 1, "sfu3"))
            ph.sig(t.matmul(pR(ch), kaps(ch), g.Sb4[:, ch, :], start=True, stop=True),
                   Tn(ci, f"rmm{ch}"))
    if STOP < 9:
        return
    @block.scalar
    def _(sc):
        for ch in R:
            ph.wait(sc, Tn(ci, "mtr3"), Tn(ci - 1, f"sq6m{ch}"))
            ph.sig(sc.copy(g.MU4[:, ch, :], pT(ch)), Tn(ci, f"mue{ch}"))

    @block.tensor
    def _(t):
        for ch in R:
            ph.wait(t, Tn(ci, "mue3"))
            ph.sig(t.transpose(pT(ch), Vsl(ch), g.ident[:]), Tn(ci, f"vtr{ch}"))

    if STOP < 10:
        return
    @block.scalar
    def _(sc):
        for ch in R:
            ph.wait(sc, Tn(ci, "vtr3"))
            ph.sig(sc.copy(g.VT4[:, ch, :], pT(ch)), Tn(ci, f"vte{ch}"))

    @block.vector
    def _(v):
        for ch in R:
            ph.wait(v, Tn(ci, f"vte{ch}"))
            if STOP < 10.5:
                continue
            ph.wait(v, Tn(ci, "rmm3"), Tn(ci - 1, f"apm0{ch}"),
                    Tn(ci - 1, "dbgRt"))
            ph.sig(v.tensor_sub(g.Rt4[:, ch, :], g.VT4[:, ch, :], pR(ch)),
                   Tn(ci, f"R{ch}"))

    if STOP < 11:
        return
    SQJ = int(os.environ.get("SQJ", "6"))
    SQP = int(os.environ.get("SQP", "4"))
    for j in range(1, SQJ + 1):
        @block.tensor
        def _(t, j=j):
            for ch in R:
                Pt = MTs(ch) if j == 1 else g.PT4[:, j - 2, ch, :]
                if j == 1:
                    ph.wait(t, Tn(ci, "mue3"), Tn(ci, "bndc3"),
                            Tn(ci - 1, "pt63"))
                else:
                    ph.wait(t, Tn(ci, f"pt{j-1}3"), Tn(ci, f"mu{j-1}3"))
                i = t.matmul(pSq(ch), g.MU4[:, ch, :], Pt, start=True, stop=True)
                if j < 6 and SQP >= 3:
                    i = t.matmul(pSq2(ch), Pt, g.MU4[:, ch, :], start=True, stop=True)
                ph.sig(i, Tn(ci, f"sq{j}m{ch}"))
        @block.vector
        def _(v, j=j):
            for ch in R:
                if SQP < 2:
                    continue
                ph.wait(v, Tn(ci, f"sq{j}m3"))
                ph.sig(v.tensor_copy(g.PT4[:, j - 1, ch, :], pSq(ch)),
                       Tn(ci, f"pt{j}{ch}"))
        if j < 6 and SQP >= 4:
            @block.scalar
            def _(sc, j=j):
                for ch in R:
                    ph.wait(sc, Tn(ci, f"sq{j}m3"))
                    ph.sig(sc.copy(g.MU4[:, ch, :], pSq2(ch)),
                           Tn(ci, f"mu{j}{ch}"))

    if STOP < 12:
        return
    for pw in range(6, -1, -1):
        @block.tensor
        def _(t, pw=pw):
            for ch in R:
                lhs = MTs(ch) if pw == 0 else g.PT4[:, pw - 1, ch, :]
                w = [Tn(ci, f"R{ch}")]
                if pw >= 1:
                    w.append(Tn(ci, f"pt{pw}{ch}"))
                w.append(Tn(ci, f"ap{pw+1}3") if pw < 6 else Tn(ci - 1, "ap03"))
                ph.wait(t, *w)
                ph.sig(t.matmul(pAp(ch), lhs, g.Rt4[:, ch, :], start=True, stop=True),
                       Tn(ci, f"apm{pw}{ch}"))
        @block.vector
        def _(v, pw=pw):
            for ch in R:
                ph.wait(v, Tn(ci, f"apm{pw}3"))
                ph.sig(v.tensor_sub(g.Rt4[:, ch, :], g.Rt4[:, ch, :], pAp(ch)),
                       Tn(ci, f"ap{pw}{ch}"))

    if STOP < 13:
        return
    @block.vector
    def _(v):
        for ch in R:
            ph.wait(v, Tn(ci, f"ap0{ch}"), Tn(ci - 1, f"smm{ch}"), Tn(ci - 1, "dbgU"))
            ph.sig(v.tensor_scalar_mul(g.Ut4[:, ch, :], g.Rt4[:, ch, :],
                                       g.bcs[:, ch, ci:ci + 1]), Tn(ci, f"U{ch}"))

    @block.tensor
    def _(t):
        for ch in R:
            ph.wait(t, Tn(ci, f"U{ch}"), Tn(ci, f"cqb{ch}"), Tn(ci - 1, f"sb{ch}"),
                    Tn(ci - 1, "oev3"))
            t.matmul(pO(ch), qts(ch), g.Sb4[:, ch, :], start=True, stop=False)
            ph.sig(t.matmul(pO(ch), g.CqTb[:, ch, :], g.Ut4[:, ch, :],
                            start=False, stop=True), Tn(ci, f"omm{ch}"))
        for ch in R:
            ph.wait(t, Tn(ci, "vte3"), Tn(ci, f"kqb{ch}"))
            ph.sig(t.transpose(pT(ch), kbs(ch), g.ident[:]), Tn(ci, f"ktr{ch}"))

    @block.scalar
    def _(sc):
        for ch in R:
            ph.wait(sc, Tn(ci, "ktr3"), Tn(ci - 1, f"smm{ch}"))
            ph.sig(sc.copy(g.KBT4[:, ch, :], pT(ch)), Tn(ci, f"kbte{ch}"))

    @block.tensor
    def _(t):
        for ch in R:
            ph.wait(t, Tn(ci, f"kbte{ch}"), Tn(ci, f"U{ch}"), Tn(ci, "R3"))
            ph.sig(t.matmul(pR(ch), g.KBT4[:, ch, :], g.Ut4[:, ch, :],
                            start=True, stop=True), Tn(ci, f"smm{ch}"))

    if STOP < 14:
        return
    @block.vector
    def _(v):
        for ch in R:
            ph.wait(v, Tn(ci, "smm3"), Tn(ci - 1, f"sb{ch}"), Tn(ci - 1, "dbgS"))
            ph.sig(v.scalar_tensor_tensor(g.Sf4[:, ch, :], g.Sf4[:, ch, :],
                                          g.ELcf[:, s2, ch, 0:1], pR(ch),
                                          AL.mult, AL.add), Tn(ci, f"sfu{ch}"))
            ph.wait(v, Tn(ci, "omm3"), Tn(ci - 1, f"sqa{ch}"))
            ph.sig(v.tensor_copy(g.Osb[:, ch, :], pO(ch)), Tn(ci, f"oev{ch}"))

    @block.gpsimd
    def _(gp):
        for ch in R:
            ph.wait(gp, Tn(ci, f"sfu{ch}"), Tn(ci, f"omm{ch}"))
            ph.sig(gp.tensor_copy(g.Sb4[:, ch, :], g.Sf4[:, ch, :]), Tn(ci, f"sb{ch}"))

    if STOP < 15:
        return
    @block.scalar
    def _(sc):
        for ch in R:
            ph.wait(sc, Tn(ci, f"oev{ch}"))
            ph.sig(sc.activation(g.TRS[:], g.Osb[:, ch, :], ACT.Square,
                                 accum_out=g.rms[:, ch, 0:1]), Tn(ci, f"sqa{ch}"))
            sc.activation(g.rms[:, ch, 1:2], g.rms[:, ch, 0:1], ACT.Ln,
                          scale=1.0 / 128, bias=1e-5)
            ph.sig(sc.activation(g.rms[:, ch, 2:3], g.rms[:, ch, 1:2], ACT.Exp,
                                 scale=-0.5), Tn(ci, f"rstd{ch}"))

    @block.vector
    def _(v):
        for ch in R:
            ph.wait(v, Tn(ci, f"rstd{ch}"), Tn(ci, "sgld"), Tn(ci - 2, "ost"),
                    Tn(ci, "dbgO"))
            v.tensor_scalar_mul(g.Osb[:, ch, :], g.Osb[:, ch, :], g.rms[:, ch, 2:3])
            v.tensor_mul(g.Osb[:, ch, :], g.Osb[:, ch, :], g.nw[:])
            ph.sig(v.tensor_mul(g.obuf[:, s2, ch * 128:ch * 128 + C],
                                g.Osb[:, ch, :], g.sgbuf[:, s2, ch, :]),
                   Tn(ci, f"epi{ch}"))

    @block.scalar
    def _(sc):
        ph.wait(sc, *[Tn(ci, f"epi{ch}") for ch in R])
        ph.sigd(sc.dma_start(out=g.o_o[ci], in_=g.obuf[:, s2, :]), Tn(ci, "ost"))

    if dbg:
        @block.sync
        def _(sy):
            for name, ap in dbg.items():
                tag, src = {
                    "L": (Tn(ci, "args0"), Lsl(0)),
                    "CkT": (Tn(ci, "MT0"), g.CkT[:, 0, :]),
                    "CqT": (Tn(ci, "cqb0"), g.CqT[:, 0, :]),
                    "Rt": (Tn(ci, "R0"), g.Rt4[:, 0, :]),
                    "U": (Tn(ci, "U0"), g.Ut4[:, 0, :]),
                    "S": (Tn(ci, "sfu0"), g.Sf4[:, 0, :]),
                    "O": (Tn(ci, "oev0"), g.Osb[:, 0, :]),
                }[name]
                ph.wait(sy, tag)
                ph.sigd(sy.dma_start(out=ap[ci], in_=src), Tn(ci, f"dbg{name}"))



D = 2048
TL = 1024
NOC = 16


def build_launch_C():
    nc = bass.Bass("TRN2", target_bir_lowering=False, debug=False)
    ph = P(nc)
    i_oT = nc.dram_tensor("i_oT", [D, TL], BF16, kind="ExternalInput").ap()
    i_acc = nc.dram_tensor("i_acc", [D, TL], F32, kind="ExternalInput").ap()
    i_woT = nc.dram_tensor("i_woT", [D, D], F32, kind="ExternalInput").ap()
    o_out = nc.dram_tensor("o_out", [NOC, 128, TL], F32, kind="ExternalOutput").ap()
    fm = lambda ap: ap.rearrange("(c p) t -> p c t", p=128)

    with contextlib.ExitStack() as es:
        E = es.enter_context
        ot = E(nc.sbuf_tensor("ot", [128, 16, TL], BF16))
        acc = E(nc.sbuf_tensor("acc", [128, 16, TL], F32))
        wo = E(nc.sbuf_tensor("wo", [128, 16, D], BF16))
        pp = E(nc.psum_tensor("pp", [128, 8, 512], F32))
        block = E(nc.Block())

        @block.sync
        def _(sy):
            ph.sigd(sy.dma_start(out=ot[:], in_=fm(i_oT)), "oldd")
            ph.sigd(sy.dma_start(out=acc[:], in_=fm(i_acc)), "aldd")

        @block.gpsimd
        def _(gp):
            for kg in range(4):
                i = gp.dma_start(out=wo[:, 4 * kg:4 * kg + 4, :],
                                 in_=fm(i_woT)[:, 4 * kg:4 * kg + 4, :])
                ph.sigd(i, f"w{kg}")

        @block.tensor
        def _(t):
            ph.wait(t, "oldd", "w0", "w1", "w2", "w3")
            for u in range(NOC * 2):
                oc, th = u // 2, u % 2
                bank = u % 8
                ph.wait(t, f"ev{u - 8}")
                for kc in range(16):
                    i = t.matmul(pp.ap()[:, bank, :], wo[:, kc, oc * 128:oc * 128 + 128],
                                 ot[:, kc, th * 512:th * 512 + 512],
                                 start=(kc == 0), stop=(kc == 15))
                ph.sig(i, f"mm{u}")

        @block.vector
        def _(v):
            ph.wait(v, "aldd")
            for u in range(NOC * 2):
                oc, th = u // 2, u % 2
                bank = u % 8
                ph.wait(v, f"mm{u}")
                i = v.tensor_add(acc[:, oc, th * 512:th * 512 + 512],
                                 pp.ap()[:, bank, :],
                                 acc[:, oc, th * 512:th * 512 + 512])
                ph.sig(i, f"ev{u}")

        @block.sync
        def _(sy):
            for oc in range(NOC):
                ph.wait(sy, f"ev{2 * oc}", f"ev{2 * oc + 1}")
                ph.sigd(sy.dma_start(out=o_out[oc], in_=acc[:, oc, :]), "ost")
            ph.wait(sy, "ost")

        @block.gpsimd
        def _(gp):
            ph.wait(gp, "ost")
    return nc



def prep_inputs_A(inp, core):
    b, q = core // 4, core % 4
    x = np.asarray(inp["L0_attn_hidden_states"])
    r0 = q * 1024
    xs = x[b, max(0, r0 - 3):r0 + 1024]
    if r0 == 0:
        xs = np.concatenate([np.zeros((3, 2048), np.float32), xs], 0)
    A = lambda a: np.ascontiguousarray(a, np.float32)
    g = lambda k: np.asarray(inp["L0_attn_" + k])
    convw = np.stack([g("conv_q").reshape(16, 128, 4).transpose(1, 0, 2),
                      g("conv_k").reshape(16, 128, 4).transpose(1, 0, 2),
                      g("conv_v").reshape(16, 128, 4).transpose(1, 0, 2)], axis=1)
    W1s = np.zeros((128, 2, 32), np.float32)
    for m, W in enumerate((g("Wb1"), g("Wa1"))):
        W1s[0, m] = W[:, 2]; W1s[32, m] = W[:, 0]; W1s[64, m] = W[:, 1]
    return {
        "xT": A(xs.T),
        "WqT": A(g("Wq").T), "WkT": A(g("Wk").T), "WvT": A(g("Wv").T),
        "convw": A(convw),
        "WsvT": A(g("Wsv").T),
        "A_log_r": A(g("A_log")[None, :]),
        "dt_b": A(g("dt_bias").reshape(16, 128).T),
        "W1r": A(W1s),
        "b1c": A(np.stack([g("bb1"), g("ba1")], 1)),
        "W2c": A(np.stack([g("Wb2")[0], g("Wa2")[0]], 1)),
        "b2r": A(np.tile(np.array([g("bb2")[0], g("ba2")[0]], np.float32)[None], (128, 1))),
        "Wg1T": A(g("Wg1").T), "Wg2T": A(g("Wg2").T),
        "bg2c": A(g("bg2").reshape(16, 128).T),
        "w1T": A(np.asarray(inp["L0_ffn_w1"]).T),
        "w2T": A(np.asarray(inp["L0_ffn_w2"]).T),
        "w3T": A(np.asarray(inp["L0_ffn_w3"]).T),
    }

BD16 = ml_dtypes.bfloat16


def _relayout_AB(aouts, inputs):
    norm_w = np.asarray(inputs["L0_attn_norm_w"], np.float32)
    i_nw = np.ascontiguousarray(np.tile(norm_w[None, :], (128, 1)))
    maps = []
    for c in range(8):
        i_k = np.empty((4, 128, 4096), BD16)
        i_q = np.empty((4, 128, 4096), BD16)
        i_v = np.empty((4, 128, 4096), BD16)
        i_g = np.empty((4, 128, 4096), np.float32)
        i_sg = np.empty((4, 128, 4096), BD16)
        i_b = np.empty((128, 4, 32), np.float32)
        for j in range(4):
            gi = 4 * c + j
            b, h = gi // 16, gi % 16
            rs = slice(h * 128, (h + 1) * 128)
            ks = [aouts[4 * b + a] for a in range(4)]
            i_k[j] = np.concatenate([np.asarray(k["o_kn"])[rs] for k in ks], 1)
            i_q[j] = np.concatenate([np.asarray(k["o_qs"])[rs] for k in ks], 1)
            i_v[j] = np.concatenate([np.asarray(k["o_v"])[rs] for k in ks], 1)
            i_g[j] = np.concatenate([np.asarray(k["o_g"])[rs] for k in ks], 1)
            bt = np.concatenate([np.asarray(k["o_beta"])[h] for k in ks], 0)
            i_b[:, j, :] = bt.reshape(32, 128).T
            sg = np.concatenate([np.asarray(k["o_sgate"])[rs] for k in ks], 1)
            i_sg[j] = sg.T.reshape(32, 128, 128).transpose(1, 0, 2).reshape(128, 4096)
        maps.append({"i_k": i_k, "i_q": i_q, "i_v": i_v, "i_g": i_g,
                     "i_b": i_b, "i_sg": i_sg, "i_nw": i_nw})
    return maps


def _relayout_BC(aouts, bouts, inputs):
    woT = np.ascontiguousarray(np.asarray(inputs["L0_attn_Wo"], np.float32).T)
    maps = []
    for c in range(8):
        b, t0 = c // 4, (c % 4) * 1024
        oT = np.empty((2048, 1024), BD16)
        for h in range(16):
            gi = b * 16 + h
            bc_core, j = gi // 4, gi % 4
            och = np.asarray(bouts[bc_core]["o_o"])[:, :, j * 128:(j + 1) * 128]
            oT[h * 128:(h + 1) * 128, :] = och.reshape(4096, 128)[t0:t0 + 1024].T
        maps.append({"i_oT": oT, "i_acc": np.asarray(aouts[c]["o_acc"]),
                     "i_woT": woT})
    return maps


_PROGS = {}


def kernel(**inputs):
    if "A" not in _PROGS:
        _PROGS["A"] = build_launch_A()
        _PROGS["B"] = build_launch_B()
        _PROGS["C"] = build_launch_C()
    ams = [prep_inputs_A(inputs, a) for a in range(8)]
    aouts = run_bass_kernel_spmd(_PROGS["A"], ams, core_ids=list(range(8))).results
    bms = _relayout_AB(aouts, inputs)
    bouts = run_bass_kernel_spmd(_PROGS["B"], bms, core_ids=list(range(8))).results
    cms = _relayout_BC(aouts, bouts, inputs)
    couts = run_bass_kernel_spmd(_PROGS["C"], cms, core_ids=list(range(8))).results
    out = np.empty((2, 4096, 2048), np.float32)
    for c in range(8):
        b, t0 = c // 4, (c % 4) * 1024
        oo = np.asarray(couts[c]["o_out"], np.float32)
        out[b, t0:t0 + 1024, :] = oo.transpose(2, 0, 1).reshape(1024, 2048)
    return out
